# version 02 (final)
# baseline (speedup 1.0000x reference)
import numpy as np
import concourse.bass as bass
import concourse.mybir as mybir
from concourse.bass_utils import run_bass_kernel_spmd
from concourse.alu_op_type import AluOpType as OP

F32 = mybir.dt.float32
F32R = mybir.dt.float32
AF = mybir.ActivationFunctionType

B, T, D = 4, 4096, 1024
H, DK, DV = 8, 64, 128
RL = 64
FF = 4096
MH = 1024
NE = 8
SUB = 512
NSUB = T // SUB
CH = 64
NCH = SUB // CH
DEC = 0.9
NPT = D // 128
STAGE_LIMIT = 99
DEBUG_TAPS = ()
DELTA_DBG = False


def _fm(v):
    return np.ascontiguousarray(np.asarray(v, np.float32).reshape(-1, 128).T)


def kernel(**inputs):
    x = np.asarray(inputs["L0_attn_x"], dtype=np.float32)
    shared = _weights(inputs)
    k = _Kern()
    k.build()
    in_maps = []
    for c in range(8):
        m = dict(shared)
        m["x"] = np.ascontiguousarray(x[c % 4].T)
        in_maps.append(m)
    res = run_bass_kernel_spmd(k.nc, in_maps, core_ids=list(range(8)))
    return np.stack([res.results[c]["out"].T for c in range(4)], 0).astype(np.float32)


def _weights(inputs):
    g = lambda n: np.asarray(inputs[n], np.float32)
    m = {}
    m["w1_shared"] = g("L0_ffn_w1_shared")
    m["w_shared_in"] = g("L0_ffn_w_shared_in")
    m["w_shared_out"] = g("L0_ffn_w_shared_out")
    m["w2"] = np.ascontiguousarray(g("L0_ffn_w2").reshape(NE * MH, D))
    m["w_gate"] = g("L0_ffn_w_gate")
    m["gate_bias"] = g("L0_ffn_gate_bias").reshape(NE, 1)
    m["conv_w"] = np.ascontiguousarray(
        g("L0_attn_conv_w").reshape(NPT, 128, 4).transpose(1, 0, 2).reshape(128, NPT * 4))
    m["conv_b"] = _fm(g("L0_attn_conv_b"))
    m["w_norm_in"] = _fm(g("L0_attn_w_norm_in"))
    m["wq"] = g("L0_attn_wq")
    m["wk"] = g("L0_attn_wk")
    m["wv"] = g("L0_attn_wv")
    m["w_out"] = g("L0_attn_w_out")
    m["w_spike"] = g("L0_attn_w_spike")
    m["w_au"] = g("L0_attn_w_au")
    m["b_au"] = g("L0_attn_b_au").reshape(RL, 1)
    m["w_ad"] = g("L0_attn_w_ad")
    m["b_ad"] = _fm(g("L0_attn_b_ad"))
    m["aspk"] = _fm(g("L0_attn_alpha_spike").reshape(512))
    m["bspk"] = _fm(g("L0_attn_beta_spike").reshape(512))
    m["w_beta"] = g("L0_attn_w_beta")
    m["b_beta"] = g("L0_attn_b_beta").reshape(H, 1)
    m["w_hn"] = np.ascontiguousarray(g("L0_attn_w_hn").T)
    m["w_u1"] = g("L0_attn_w_u1")
    m["b_u1"] = g("L0_attn_b_u1").reshape(RL, 1)
    m["w_u2"] = g("L0_attn_w_u2")
    m["b_u2"] = _fm(g("L0_attn_b_u2"))
    m["w_ffn"] = _fm(g("L0_attn_w_ffn"))
    m["w_ff1"] = g("L0_attn_w_ff1")
    m["w_ff3"] = g("L0_attn_w_ff3")
    m["w_ff2"] = g("L0_attn_w_ff2")
    m["ident"] = np.eye(128, dtype=np.float32)
    m["ltmask"] = np.tril(np.ones((64, 64), np.float32), -1)
    m["ulmask"] = np.triu(np.ones((64, 64), np.float32), 0)
    tau = np.arange(SUB) % CH
    m["powt"] = np.broadcast_to(
        (DEC ** (-tau.astype(np.float64))).astype(np.float32), (128, SUB)).copy()
    m["onesc"] = np.ones((128, 1), np.float32)
    m["onesr"] = np.ones((1, 128), np.float32)
    bea = np.zeros((1, 128), np.float32); bea[0, 0:64] = 1.0
    beb = np.zeros((1, 128), np.float32); beb[0, 64:128] = 1.0
    m["bexpA"] = bea
    m["bexpB"] = beb
    b2t = np.zeros((128, 8), np.float32)
    for p in range(4):
        b2t[0:64, 2 * p] = 1.0
        b2t[64:128, 2 * p + 1] = 1.0
    m["blk2T"] = b2t
    return m


class _Kern:
    def __init__(self):
        self.nc = bass.Bass("TRN2", target_bir_lowering=False)
        self.sb = {}
        self.dr = {}
        self.cnt = None
        self.cur = None
        self.eng = None
        self.bank_guard = [None] * 8

    def build(self):
        nc = self.nc
        dr = self.dr
        dr["x"] = nc.dram_tensor("x", [D, T], F32, kind="ExternalInput").ap()
        names_r = ["w1_shared", "w_shared_in", "w_shared_out", "w2", "wq", "wk",
                   "wv", "w_out", "w_spike", "w_au", "w_ad", "w_u1", "w_u2",
                   "w_ff1", "w_ff3", "w_ff2"]
        shapes = {"w1_shared": [D, 2 * MH], "w_shared_in": [D, 2 * MH],
                  "w_shared_out": [MH, D], "w2": [NE * MH, D], "wq": [D, 512],
                  "wk": [D, 512], "wv": [D, 1024], "w_out": [D, D],
                  "w_spike": [D, 512], "w_au": [D, RL], "w_ad": [RL, 512],
                  "w_u1": [D, RL], "w_u2": [RL, D], "w_ff1": [D, FF],
                  "w_ff3": [D, FF], "w_ff2": [FF, D]}
        for n in names_r:
            dr[n] = nc.dram_tensor(n, shapes[n], F32R, kind="ExternalInput").ap()
        f32ins = {"w_gate": [D, NE], "gate_bias": [NE, 1], "conv_w": [128, NPT * 4],
                  "conv_b": [128, NPT], "w_norm_in": [128, NPT], "b_au": [RL, 1],
                  "b_ad": [128, 4], "aspk": [128, 4], "bspk": [128, 4],
                  "w_beta": [D, H], "b_beta": [H, 1], "w_hn": [128, H],
                  "b_u1": [RL, 1], "b_u2": [128, NPT], "w_ffn": [128, NPT],
                  "ident": [128, 128], "ltmask": [64, 64], "ulmask": [64, 64],
                  "powt": [128, SUB], "onesc": [128, 1],
                  "onesr": [1, 128], "bexpA": [1, 128], "bexpB": [1, 128],
                  "blk2T": [128, 8]}
        for n, s in f32ins.items():
            dr[n] = nc.dram_tensor(n, s, F32, kind="ExternalInput").ap()
        dr["out"] = nc.dram_tensor("out", [D, T], F32, kind="ExternalOutput").ap()
        for nm in DEBUG_TAPS:
            pass
        scr = lambda n, s, dt: nc.dram_tensor(n, s, dt, kind="Internal").ap()
        dr["d_xh"] = scr("d_xh", [D, T], F32R)
        dr["d_xha"] = scr("d_xha", [D, T], F32R)
        dr["d_xh2"] = scr("d_xh2", [D, T], F32R)
        dr["d_wd"] = scr("d_wd", [NE, T], F32)
        dr["d_hid"] = scr("d_hid", [MH, T], F32R)
        dr["d_xa"] = scr("d_xa", [D, T], F32)
        dr["d_k"] = scr("d_k", [512, T], F32R)
        dr["d_q"] = scr("d_q", [512, T], F32R)
        dr["d_sd"] = scr("d_sd", [512, T], F32)
        dr["d_ab"] = scr("d_ab", [512, T], F32)
        dr["d_bb"] = scr("d_bb", [H, T], F32)
        dr["d_spk"] = scr("d_spk", [512, T], F32)
        dr["d_v"] = scr("d_v", [128, H * T], F32R)
        dr["d_y"] = scr("d_y", [128, H * T], F32)
        dr["d_x2"] = scr("d_x2", [D, T], F32)
        dr["d_h"] = scr("d_h", [D, T], F32R)
        dr["d_kss"] = scr("d_kss", [2, 4 * T], F32)
        dr["d_qss"] = scr("d_qss", [2, 4 * T], F32)
        dr["d_x3"] = scr("d_x3", [D, T], F32)
        dr["d_xl"] = scr("d_xl", [D, T], F32R)
        for nm in ("d_dbg_kn", "d_dbg_al", "d_dbg_gt", "d_dbg_gti",
                   "d_dbg_aap", "d_dbg_bbv", "d_dbg_qtt"):
            dr[nm] = scr(nm, [512, SUB], F32)
        dr["d_dbg_vt"] = scr("d_dbg_vt", [128, H * SUB], F32)
        dr["d_dbg_ar"] = scr("d_dbg_ar", [1, H * SUB], F32)
        dr["d_dbg_br"] = scr("d_dbg_br", [1, H * SUB], F32)
        dr["d_dbg_ks"] = scr("d_dbg_ks", [1, H * SUB], F32)
        dr["d_dbg_qs"] = scr("d_dbg_qs", [1, H * SUB], F32)
        for nm in DEBUG_TAPS:
            t = dr[nm].tensor
            dr["dbg_" + nm] = nc.dram_tensor("dbg_" + nm, list(t.shape), t.dtype,
                                             kind="ExternalOutput").ap()

        AR = 47616
        arena = nc.alloc_sbuf_tensor("arena", [128, AR], F32R).ap()
        self.arena = arena
        W1, W2, RA, RB, RC, RD, RE = 0, 16384, 24576, 28672, 32768, 36864, 40960
        TL = 45056
        sb = self.sb
        def VR(name, off, ln, p0=0, p1=128):
            sb[name] = arena[p0:p1, off:off + ln]
        def VF(name, off, ln, p0=0, p1=128):
            sb[name] = arena[p0:p1, off:off + ln].bitcast(F32)
        VF("bigA", RA, 4096)
        VR("hatR", RB, 4096)
        VR("hidR", RC, 4096)
        VR("wbufA", W1, 16384)
        VR("wbufB", W2, 8192)
        VF("zD", RD, 4096)
        VR("vsub", RE, 4096)
        VF("gsub", RE, 4096)
        VR("ksub", W1 + 0, 2048)
        VR("qsub", W1 + 2048, 2048)
        VR("kprj", W2 + 0, 2048)
        VR("qprj", W2 + 2048, 2048)
        VF("xlsub", W2 + 0, 4096)
        VF("absub", W1 + 4096, 2048)
        VF("spk", W1 + 6144, 2048)
        VF("gt", W1 + 8192, 2048)
        VF("gti", W1 + 10240, 2048)
        VF("Lt", W1 + 12288, 2048)
        VR("aap", W1 + 14336, 2048)
        VR("aapn", W2 + 0, 2048)
        VR("bbv", W2 + 2048, 2048)
        VR("qtt", W2 + 4096, 2048)
        VR("bd_ap", W2 + 6144, 512)
        VR("bd_apn", W2 + 6656, 512)
        VR("bd_b", W2 + 7168, 512)
        VR("bd_bt", W2 + 7680, 512)
        VF("psc", W1 + 0, 2048)
        VF("ctil", W1 + 2048, 2048)
        VF("sdS", W1 + 8192, 2048)
        VF("cacc", RE + 0, 4096)
        VF("sdP", RE + 0, 2048)
        VF("abP", RE + 2048, 2048)
        VR("kP", RD + 0, 2048)
        VR("qP", RD + 2048, 2048)
        VF("kssA", RA + 0, 2048, 0, 2)
        VF("qssA", RA + 2048, 2048, 0, 2)
        VF("kssE", 28672 + 0, 2048, 0, 1)
        VF("kssO", 28672 + 2048, 2048, 0, 1)
        VF("kssEO", 28672 + 0, 4096, 0, 1)
        VF("qssE", 32768 + 0, 2048, 0, 1)
        VF("qssO", 32768 + 2048, 2048, 0, 1)
        VF("qssEO", 32768 + 0, 4096, 0, 1)
        VF("ysub", RD, 4096)
        VR("V0", TL + 0, 128)
        VR("V0T", TL + 128, 128)
        VR("Vk", TL + 256, 128)
        VR("VkT", TL + 384, 128)
        VR("Uk", TL + 512, 128)
        VR("Uk2", TL + 640, 128)
        VR("Mp1", TL + 768, 64)
        VR("Mp2", TL + 832, 64)
        VR("qp1", TL + 896, 64)
        VR("qp2", TL + 960, 64)
        VF("arow", 28672, 4096, 0, 1)
        VF("brow", 32768, 4096, 0, 1)
        VF("wdrow", 36864, 512, 0, 1)
        VF("s1", TL + 1024, 512)
        VF("s2", TL + 1536, 512)
        VF("s3", TL + 2048, 512)
        P = lambda n, s, dt: self.sb.__setitem__(
            n, nc.alloc_sbuf_tensor("sb_" + n, s, dt).ap())
        for n, s in [("ident", [128, 128]), ("lt", [64, 64]), ("ul", [64, 64]),
                     ("powt", [128, SUB]),
                     ("onesc", [128, 1]), ("onesr", [1, 128]),
                     ("bexpA", [1, 128]), ("bexpB", [1, 128]),
                     ("blk2T", [128, 8]), ("vsm", [128, 80]),
                     ("slhs", [128, 64]),
                     ("bau", [RL, 1]), ("bu1", [RL, 1]), ("bbeta", [H, 1]),
                     ("gbias", [NE, 1]), ("whn", [128, H]),
                     ("Gst", [128, 8]), ("pcar", [128, 4]),
                     ("halo", [128, NPT * 3]),
                     ("row1", [1, SUB]), ("row2", [1, SUB]), ("row3", [1, SUB]),
                     ("r8a", [NE, SUB]), ("r8b", [NE, SUB]),
                     ("r8c", [NE, SUB]), ("r8d", [NE, SUB]),
                     ("sjunk", [1, 64]), ("ceps5", [128, 1]), ("ceps6", [128, 1]),
                     ("zcol", [128, 1]), ("ceps11", [128, 1])]:
            P(n, s, F32)
        P("Spair", [128, 4 * 128], F32R)

        self.pb = [nc.alloc_psum_tensor(f"psb{i}", [128, 512], F32).ap()
                   for i in range(8)]

        semn = ["ld", "sp", "mm", "vv", "sc"]
        self._semctx = [nc.semaphore(s) for s in semn]
        self.sems = {s: cm.__enter__() for s, cm in zip(semn, self._semctx)}

        blk = nc.Block()
        block = blk.__enter__()

        @block.sync
        def _(eng):
            self._run("sync", eng)

        @block.gpsimd
        def _(eng):
            self._run("gpsimd", eng)

        @block.tensor
        def _(eng):
            self._run("tensor", eng)

        @block.vector
        def _(eng):
            self._run("vector", eng)

        @block.scalar
        def _(eng):
            self._run("scalar", eng)

        blk.__exit__(None, None, None)
        for cm in self._semctx:
            cm.__exit__(None, None, None)

    def _run(self, engine, eng):
        self.cur = engine
        self.eng = eng
        self.cnt = {k: 0 for k in self.sems}
        self.waited = {e: {k: 0 for k in self.sems} for e in
                       ["sync", "gpsimd", "tensor", "vector", "scalar"]}
        self.bank_guard = [None] * 8
        self._script()

    def on(self, name):
        return self.cur == name

    def w(self, engine, sem, val):
        if val is None:
            return
        if self.waited[engine][sem] >= val:
            return
        self.waited[engine][sem] = val
        if self.on(engine):
            self.eng.wait_ge(self.sems[sem], val)

    def inc(self, sem, by=1):
        self.cnt[sem] += by
        return self.cnt[sem]

    def load(self, dst, src):
        n = self.inc("ld", 16)
        if self.on("sync"):
            dst = self._match3(dst, src)
            self.eng.dma_start(out=dst, in_=src).then_inc(self.sems["ld"], 16)
        return n

    def store(self, dst, src, after_vv=None, after_sc=None, accum=False):
        self.w("gpsimd", "vv", after_vv if after_vv is not None else self.cnt["vv"])
        if after_sc:
            self.w("gpsimd", "sc", after_sc)
        n = self.inc("sp", 16)
        if self.on("gpsimd"):
            src = self._match3(src, dst)
            if accum:
                self.eng.dma_start(out=dst, in_=src, accum_op=OP.add).then_inc(
                    self.sems["sp"], 16)
            else:
                self.eng.dma_start(out=dst, in_=src).then_inc(self.sems["sp"], 16)
        return n

    def mm(self, bank, out, lhsT, rhs, start, stop, inc=False, tr=False):
        if start and bank is not None and self.bank_guard[bank]:
            sem, c = self.bank_guard[bank]
            self.w("tensor", sem, c)
            self.bank_guard[bank] = None
        n = self.inc("mm") if (inc or stop) else None
        if self.on("tensor"):
            ins = self.eng.matmul(out, lhsT, rhs, start=start, stop=stop,
                                  is_transpose=True if tr else None)
            if n is not None:
                ins.then_inc(self.sems["mm"], 1)
        return n

    def vo(self, f, inc=False, reads_bank=None):
        n = self.inc("vv") if inc else None
        if self.on("vector"):
            ins = f(self.eng)
            if n is not None:
                ins.then_inc(self.sems["vv"], 1)
        if reads_bank is not None:
            if n is None:
                n = self.inc("vv")
                if self.on("vector"):
                    self.eng.engine_nop().then_inc(self.sems["vv"], 1)
            for b in (reads_bank if isinstance(reads_bank, (list, tuple)) else [reads_bank]):
                self.bank_guard[b] = ("vv", n)
        return n

    def so(self, f, inc=False, reads_bank=None):
        n = self.inc("sc") if inc else None
        if self.on("scalar"):
            ins = f(self.eng)
            if n is not None:
                ins.then_inc(self.sems["sc"], 1)
        if reads_bank is not None:
            if n is None:
                n = self.inc("sc")
                if self.on("scalar"):
                    self.eng.copy(self.sb["sjunk"][:], self.sb["sjunk"][:]).then_inc(
                        self.sems["sc"], 1)
            for b in (reads_bank if isinstance(reads_bank, (list, tuple)) else [reads_bank]):
                self.bank_guard[b] = ("sc", n)
        return n

    def stage_barrier(self):
        self.w("sync", "vv", self.cnt["vv"])
        self.w("sync", "sc", self.cnt["sc"])
        self.w("sync", "mm", self.cnt["mm"])
        self.w("sync", "sp", self.cnt["sp"])
        self.w("gpsimd", "sp", self.cnt["sp"])

    def vnop(self):
        if self.on("vector"):
            self.eng.engine_nop()

    def snop(self):
        if self.on("scalar"):
            self.eng.copy(self.sb["sjunk"][:], self.sb["sjunk"][:])

    def dsub(self, d, s, npt=NPT, sub=SUB):
        return d[:, s * sub:(s + 1) * sub].rearrange("(p r) t -> r p t", r=128)

    @staticmethod
    def _match3(a, b):
        if len(b.shape) == 3 and len(a.shape) == 2:
            a = a.rearrange("r (p t) -> r p t", p=b.shape[1])
        return a

    def _script(self):
        sb, dr = self.sb, self.dr
        ld = 0
        for n, d in [("ident", "ident"), ("lt", "ltmask"), ("ul", "ulmask"),
                     ("powt", "powt"), ("onesc", "onesc"),
                     ("onesr", "onesr"), ("bexpA", "bexpA"), ("bexpB", "bexpB"),
                     ("blk2T", "blk2T"), ("bau", "b_au"), ("bu1", "b_u1"),
                     ("bbeta", "b_beta"), ("gbias", "gate_bias"), ("whn", "w_hn")]:
            ld = self.load(sb[n][:], dr[d][:])
        ld = self.load(sb["vsm"][:, 0:8], dr["w_norm_in"][:])
        ld = self.load(sb["vsm"][:, 8:16], dr["conv_b"][:])
        ld = self.load(sb["vsm"][:, 16:48], dr["conv_w"][:])
        ld = self.load(sb["vsm"][:, 48:52], dr["b_ad"][:])
        ld = self.load(sb["vsm"][:, 52:56], dr["aspk"][:])
        ld = self.load(sb["vsm"][:, 56:60], dr["bspk"][:])
        ld = self.load(sb["vsm"][:, 60:68], dr["b_u2"][:])
        ld = self.load(sb["vsm"][:, 72:80], dr["w_ffn"][:])
        self.w("vector", "ld", ld)
        self.vo(lambda v: v.memset(sb["Gst"][:], 0.0))
        self.vo(lambda v: v.memset(sb["pcar"][:], 0.0))
        self.vo(lambda v: v.memset(sb["Spair"][:].bitcast(F32), 0.0))
        self.vo(lambda v: v.memset(sb["bd_ap"][:].bitcast(F32), 0.0))
        self.vo(lambda v: v.memset(sb["bd_apn"][:].bitcast(F32), 0.0))
        self.vo(lambda v: v.memset(sb["bd_b"][:].bitcast(F32), 0.0))
        self.vo(lambda v: v.memset(sb["Mp1"][:].bitcast(F32), 0.0))
        self.vo(lambda v: v.memset(sb["Mp2"][:].bitcast(F32), 0.0))
        self.vo(lambda v: v.memset(sb["qp1"][:].bitcast(F32), 0.0))
        self.vo(lambda v: v.memset(sb["qp2"][:].bitcast(F32), 0.0))
        self.vo(lambda v: v.memset(sb["halo"][:], 0.0))
        self.vo(lambda v: v.memset(sb["slhs"][:], 0.0))
        self.vo(lambda v: v.memset(sb["ceps5"][:], 1e-5))
        self.vo(lambda v: v.memset(sb["ceps6"][:], 1e-6))
        self.vo(lambda v: v.memset(sb["zcol"][:], 0.0))
        self.vo(lambda v: v.memset(sb["ceps11"][:], 1e-11))
        self.vo(lambda v: v.tensor_scalar_mul(sb["vsm"][:, 64:72],
                                              sb["vsm"][:, 60:68], 0.5))
        self._build_sumlhs()
        self.vo(lambda v: v.engine_nop(), inc=True)

        stage = 0
        if stage < STAGE_LIMIT: self.st_lnprep(dr["x"], dr["d_xh"], dr["d_wd"])
        stage += 1
        if stage < STAGE_LIMIT: self.st_moehid(dr["d_xh"], dr["d_hid"])
        stage += 1
        if stage < STAGE_LIMIT: self.st_moeshared(dr["d_xh"], dr["x"], dr["d_xa"])
        stage += 1
        if stage < STAGE_LIMIT: self.st_moerouted(dr["d_hid"], dr["d_wd"], dr["d_xa"])
        stage += 1
        if stage < STAGE_LIMIT: self.st_lnrms(dr["d_xa"], dr["d_xl"], dr["d_xha"])
        stage += 1
        if stage < STAGE_LIMIT: self.st_projA()
        stage += 1
        if stage < STAGE_LIMIT: self.st_projB()
        stage += 1
        if stage < STAGE_LIMIT: self.st_scan()
        stage += 1
        if stage < STAGE_LIMIT: self.st_delta()
        stage += 1
        if stage < STAGE_LIMIT: self.st_postattn()
        stage += 1
        if stage < STAGE_LIMIT: self.st_rmsprep(dr["d_xha"], dr["d_xh2"], wname="w_ffn", wcol=None)
        stage += 1
        if stage < STAGE_LIMIT: self.st_ffnattn()
        stage += 1
        if stage < STAGE_LIMIT: self.st_lnprep(dr["d_x3"], dr["d_xh"], dr["d_wd"])
        stage += 1
        if stage < STAGE_LIMIT: self.st_moehid(dr["d_xh"], dr["d_hid"])
        stage += 1
        if stage < STAGE_LIMIT: self.st_moeshared(dr["d_xh"], dr["d_x3"], dr["out"])
        stage += 1
        if stage < STAGE_LIMIT: self.st_moerouted(dr["d_hid"], dr["d_wd"], dr["out"])
        self.w("gpsimd", "sp", self.cnt["sp"])
        for nm in DEBUG_TAPS:
            self.store(self.dr["dbg_" + nm][:, :], self.dr[nm][:, :])
        self.w("gpsimd", "sp", self.cnt["sp"])
        self.w("sync", "sp", self.cnt["sp"])

    def st_lnprep(self, src_d, xh_d, wd_d):
        sb, dr, pb = self.sb, self.dr, self.pb
        self.stage_barrier()
        ldg = 0
        for kc in range(8):
            ldg = self.load(sb["wbufB"][:, kc * 8:(kc + 1) * 8].bitcast(F32),
                            dr["w_gate"][kc * 128:(kc + 1) * 128, :])
        for s in range(NSUB):
            self.stage_barrier()
            lds = self.load(sb["bigA"][:], self.dsub(src_d, s))
            self.w("scalar", "ld", lds)
            self.w("tensor", "ld", lds)
            for p in range(NPT):
                self.mm(6 if p == 0 else None, pb[6][0:1, 0:SUB], sb["onesc"][:],
                        sb["bigA"][:, p * SUB:(p + 1) * SUB],
                        start=(p == 0), stop=(p == NPT - 1))
            mqs = None
            mm_hist = []
            for p in range(NPT):
                sl = "s1" if p % 2 == 0 else "s2"
                if p >= 2:
                    self.w("scalar", "mm", mm_hist[p - 2])
                scs = self.so(lambda sc, p=p, sl=sl: sc.activation(
                    sb[sl][:], sb["bigA"][:, p * SUB:(p + 1) * SUB], AF.Square),
                    inc=True)
                self.w("tensor", "sc", scs)
                mqs = self.mm(7 if p == 0 else None, pb[7][0:1, 0:SUB], sb["onesc"][:],
                              sb[sl][:], start=(p == 0), stop=(p == NPT - 1), inc=True)
                mm_hist.append(mqs)
            self.w("vector", "mm", mqs)
            self.vo(lambda v: v.tensor_scalar_mul(sb["row1"][:], pb[6][0:1, 0:SUB], 1.0 / D))
            self.vo(lambda v: v.tensor_scalar_mul(sb["row3"][:], pb[7][0:1, 0:SUB], 1.0 / D),
                    reads_bank=[6, 7])
            self.vo(lambda v: v.tensor_mul(sb["row2"][:], sb["row1"][:], sb["row1"][:]))
            self.vnop()
            vv1 = self.vo(lambda v: v.tensor_sub(sb["row2"][:], sb["row3"][:], sb["row2"][:]),
                          inc=True)
            self.w("scalar", "vv", vv1)
            self.so(lambda sc: sc.activation(sb["row3"][:], sb["row2"][:], AF.Ln, bias=sb["ceps5"][0:1, :]))
            self.snop()
            sc2 = self.so(lambda sc: sc.activation(sb["row2"][:], sb["row3"][:], AF.Exp,
                                                   scale=-0.5), inc=True)
            self.w("tensor", "sc", sc2)
            m1 = self.mm(6, pb[6][:, 0:SUB], sb["onesr"][:], sb["row1"][:], True, True)
            m2 = self.mm(7, pb[7][:, 0:SUB], sb["onesr"][:], sb["row2"][:], True, True)
            self.w("vector", "mm", m2)
            vvh = None
            for p in range(NPT):
                self.vo(lambda v, p=p: v.tensor_sub(
                    sb["zD"][:, p * SUB:(p + 1) * SUB],
                    sb["bigA"][:, p * SUB:(p + 1) * SUB], pb[6][:, 0:SUB]))
            for p in range(NPT):
                vvh = self.vo(lambda v, p=p: v.tensor_mul(
                    sb["hatR"][:, p * SUB:(p + 1) * SUB],
                    sb["zD"][:, p * SUB:(p + 1) * SUB], pb[7][:, 0:SUB]),
                    inc=(p == NPT - 1))
            self.vo(lambda v: v.engine_nop(), reads_bank=[6, 7])
            self.store(self.dsub(xh_d, s), sb["hatR"][:], after_vv=vvh)
            self.w("tensor", "vv", vvh)
            self.w("tensor", "ld", ldg)
            mg = None
            for kc in range(8):
                mg = self.mm(5 if kc == 0 else None, pb[5][0:NE, 0:SUB],
                             sb["wbufB"][:, kc * 8:(kc + 1) * 8].bitcast(F32),
                             sb["hatR"][:, kc * SUB:(kc + 1) * SUB].bitcast(F32),
                             start=(kc == 0), stop=(kc == 7))
            self.w("vector", "mm", mg)
            self.vo(lambda v: v.tensor_scalar_add(sb["r8a"][:], pb[5][0:NE, 0:SUB],
                                                  sb["gbias"][:]), reads_bank=5)
            vvl0 = self.vo(lambda v: v.engine_nop(), inc=True)
            self.w("scalar", "vv", vvl0)
            sce = self.so(lambda sc: sc.activation(sb["r8b"][:], sb["r8a"][:], AF.Exp),
                          inc=True)
            self.w("tensor", "sc", sce)
            mtr = None
            for sl_ in range(4):
                mtr = self.mm(4 if sl_ == 0 else None,
                              pb[4][:, sl_ * 8:(sl_ + 1) * 8],
                              sb["r8b"][:, sl_ * 128:(sl_ + 1) * 128],
                              sb["ident"][0:NE, 0:NE], True, True, tr=True)
            self.w("vector", "mm", mtr)
            self.vo(lambda v: v.tensor_copy(sb["s1"][:, 0:32], pb[4][:, 0:32]),
                    inc=True, reads_bank=4)
            self.vnop()
            ex = lambda sl_: sb["s1"][:, sl_ * 8:(sl_ + 1) * 8]
            pr = lambda sl_: sb["s1"][:, 32 + sl_ * 8:32 + (sl_ + 1) * 8]
            ms = lambda sl_: sb["s2"][:, sl_ * 8:(sl_ + 1) * 8]
            ge = lambda sl_: sb["s2"][:, 32 + sl_ * 8:32 + (sl_ + 1) * 8]
            m1 = lambda sl_: sb["s2"][:, 64 + sl_:65 + sl_]
            m2 = lambda sl_: sb["s2"][:, 68 + sl_:69 + sl_]
            sm = lambda sl_: sb["s2"][:, 72 + sl_:73 + sl_]
            for sl_ in range(4):
                self.vo(lambda v, sl_=sl_: v.reduce_sum(sm(sl_), ex(sl_),
                                                        axis=mybir.AxisListType.X))
            for sl_ in range(4):
                self.vo(lambda v, sl_=sl_: v.reciprocal(sm(sl_), sm(sl_)))
            for sl_ in range(4):
                self.vo(lambda v, sl_=sl_: v.tensor_scalar_mul(pr(sl_), ex(sl_), sm(sl_)))
            for sl_ in range(4):
                self.vo(lambda v, sl_=sl_: v.reduce_max(m1(sl_), pr(sl_),
                                                        axis=mybir.AxisListType.X))
            for sl_ in range(4):
                self.vo(lambda v, sl_=sl_: v.tensor_single_scalar(
                    ge(sl_), pr(sl_), m1(sl_), OP.is_ge))
            for sl_ in range(4):
                self.vo(lambda v, sl_=sl_: v.scalar_tensor_tensor(
                    ms(sl_), ge(sl_), -2.0, pr(sl_), OP.mult, OP.add))
            for sl_ in range(4):
                self.vo(lambda v, sl_=sl_: v.reduce_max(m2(sl_), ms(sl_),
                                                        axis=mybir.AxisListType.X))
            for sl_ in range(4):
                self.vo(lambda v, sl_=sl_: v.tensor_single_scalar(
                    ge(sl_), pr(sl_), m2(sl_), OP.is_ge))
            vtb = None
            for sl_ in range(4):
                vtb = self.vo(lambda v, sl_=sl_: v.tensor_mul(ms(sl_), pr(sl_), ge(sl_)),
                              inc=(sl_ == 3))
            self.w("tensor", "vv", vtb)
            mtb = None
            for sl_ in range(4):
                mtb = self.mm(5 if sl_ == 0 else None,
                              pb[5][0:NE, sl_ * 128:(sl_ + 1) * 128],
                              ms(sl_), sb["ident"][:], True, True, tr=True)
            self.w("vector", "mm", mtb)
            vwd = self.vo(lambda v: v.tensor_copy(sb["r8d"][:], pb[5][0:NE, 0:SUB]),
                          inc=True, reads_bank=5)
            self.store(wd_d[:, s * SUB:(s + 1) * SUB], sb["r8d"][:], after_vv=vwd)

    def _swiglu_16(self, xin, wbuf, hout):
        sb, pb = self.sb, self.pb
        for j in range(8):
            ba, bb_ = j % 4, 4 + j % 4
            for kc in range(8):
                self.mm(ba if kc == 0 else None, pb[ba][:, 0:SUB],
                        wbuf[:, kc * 2048 + j * 128: kc * 2048 + (j + 1) * 128],
                        xin[:, kc * SUB:(kc + 1) * SUB],
                        start=(kc == 0), stop=(kc == 7))
            mmb = None
            for kc in range(8):
                mmb = self.mm(bb_ if kc == 0 else None, pb[bb_][:, 0:SUB],
                              wbuf[:, kc * 2048 + 1024 + j * 128: kc * 2048 + 1024 + (j + 1) * 128],
                              xin[:, kc * SUB:(kc + 1) * SUB],
                              start=(kc == 0), stop=(kc == 7))
            self.w("scalar", "mm", mmb)
            sl = "s1" if j % 2 == 0 else "s2"
            sc1 = self.so(lambda sc, sl=sl, ba=ba: sc.activation(
                sb[sl][:], pb[ba][:, 0:SUB], AF.Silu), inc=True, reads_bank=ba)
            self.w("vector", "sc", sc1)
            self.vo(lambda v, j=j, sl=sl, bb_=bb_: v.tensor_mul(
                hout[:, j * SUB:(j + 1) * SUB],
                sb[sl][:], pb[bb_][:, 0:SUB]), inc=True, reads_bank=bb_)

    def st_moehid(self, xh_d, hid_d):
        sb, dr, pb = self.sb, self.dr, self.pb
        self.stage_barrier()
        ldw = 0
        for kc in range(8):
            ldw = self.load(sb["wbufA"][:, kc * 2048:(kc + 1) * 2048],
                            dr["w1_shared"][kc * 128:(kc + 1) * 128, :])
        for s in range(NSUB):
            self.stage_barrier()
            lds = self.load(sb["hatR"][:], self.dsub(xh_d, s))
            self.w("tensor", "ld", max(lds, ldw))
            self._swiglu_16(sb["hatR"], sb["wbufA"], sb["hidR"])
            self.store(self.dsub(hid_d, s, npt=8), sb["hidR"][:],
                       after_vv=self.cnt["vv"])

    def st_moeshared(self, xh_d, res_d, out_d):
        sb, dr, pb = self.sb, self.dr, self.pb
        self.stage_barrier()
        ldw = 0
        for kc in range(8):
            ldw = self.load(sb["wbufA"][:, kc * 2048:(kc + 1) * 2048],
                            dr["w_shared_in"][kc * 128:(kc + 1) * 128, :])
        for kc in range(8):
            ldw = self.load(sb["wbufB"][:, kc * 1024:(kc + 1) * 1024],
                            dr["w_shared_out"][kc * 128:(kc + 1) * 128, :])
        for s in range(NSUB):
            self.stage_barrier()
            lds = self.load(sb["hatR"][:], self.dsub(xh_d, s))
            lds = self.load(sb["bigA"][:], self.dsub(res_d, s))
            self.w("tensor", "ld", max(lds, ldw))
            self._swiglu_16(sb["hatR"], sb["wbufA"], sb["hidR"])
            for oc in range(8):
                bk = oc % 4
                mo = None
                for kc in range(8):
                    mo = self.mm(bk if kc == 0 else None, pb[bk][:, 0:SUB],
                                 sb["wbufB"][:, kc * 1024 + oc * 128: kc * 1024 + (oc + 1) * 128],
                                 sb["hidR"][:, kc * SUB:(kc + 1) * SUB],
                                 start=(kc == 0), stop=(kc == 7))
                self.w("vector", "mm", mo)
                self.vo(lambda v, oc=oc, bk=bk: v.scalar_tensor_tensor(
                    sb["zD"][:, oc * SUB:(oc + 1) * SUB], pb[bk][:, 0:SUB], 0.5,
                    sb["bigA"][:, oc * SUB:(oc + 1) * SUB], OP.mult, OP.add),
                    inc=True, reads_bank=bk)
            self.store(self.dsub(out_d, s), sb["zD"][:], after_vv=self.cnt["vv"])

    def st_moerouted(self, hid_d, wd_d, out_d):
        sb, dr, pb = self.sb, self.dr, self.pb
        self.stage_barrier()
        for e in range(NE):
            self.stage_barrier()
            ldw = 0
            for kc in range(8):
                ldw = self.load(sb["wbufB"][:, kc * 1024:(kc + 1) * 1024],
                                dr["w2"][e * MH + kc * 128: e * MH + (kc + 1) * 128, :])
            for s in range(NSUB):
                self.stage_barrier()
                lds = self.load(sb["hidR"][:], self.dsub(hid_d, s, npt=8))
                lds = self.load(sb["wdrow"][:], wd_d[e:e + 1, s * SUB:(s + 1) * SUB])
                self.w("tensor", "ld", max(lds, ldw))
                mb = self.mm(6, pb[6][:, 0:SUB], sb["onesr"][:],
                             sb["wdrow"][:], True, True)
                self.w("vector", "mm", mb)
                vv = None
                for p in range(8):
                    vv = self.vo(lambda v, p=p: v.scalar_tensor_tensor(
                        sb["hatR"][:, p * SUB:(p + 1) * SUB], pb[6][:, 0:SUB], 0.5,
                        sb["hidR"][:, p * SUB:(p + 1) * SUB].bitcast(F32),
                        OP.mult, OP.mult), inc=(p == 7))
                self.vo(lambda v: v.engine_nop(), reads_bank=6)
                self.w("tensor", "vv", vv)
                for oc in range(8):
                    bk = oc % 4
                    mo = None
                    for kc in range(8):
                        mo = self.mm(bk if kc == 0 else None, pb[bk][:, 0:SUB],
                                     sb["wbufB"][:, kc * 1024 + oc * 128: kc * 1024 + (oc + 1) * 128],
                                     sb["hatR"][:, kc * SUB:(kc + 1) * SUB],
                                     start=(kc == 0), stop=(kc == 7))
                    self.w("vector", "mm", mo)
                    self.vo(lambda v, oc=oc, bk=bk: v.tensor_scalar_add(
                        sb["bigA"][:, oc * SUB:(oc + 1) * SUB], pb[bk][:, 0:SUB], 0.0),
                        inc=True, reads_bank=bk)
                self.store(self.dsub(out_d, s), sb["bigA"][:],
                           after_vv=self.cnt["vv"], accum=True)

    def st_rmsprep(self, src_d, out_d, wname, wcol):
        sb, dr, pb = self.sb, self.dr, self.pb
        self.stage_barrier()
        for s in range(NSUB):
            self.stage_barrier()
            lds = self.load(sb["bigA"][:], self.dsub(src_d, s))
            self.w("scalar", "ld", lds)
            self.w("tensor", "ld", lds)
            mqs = None
            mm_hist = []
            for p in range(NPT):
                sl = "s1" if p % 2 == 0 else "s2"
                if p >= 2:
                    self.w("scalar", "mm", mm_hist[p - 2])
                scs = self.so(lambda sc, p=p, sl=sl: sc.activation(
                    sb[sl][:], sb["bigA"][:, p * SUB:(p + 1) * SUB], AF.Square),
                    inc=True)
                self.w("tensor", "sc", scs)
                mqs = self.mm(7 if p == 0 else None, pb[7][0:1, 0:SUB], sb["onesc"][:],
                              sb[sl][:], start=(p == 0), stop=(p == NPT - 1), inc=True)
                mm_hist.append(mqs)
            self.w("scalar", "mm", mqs)
            self.so(lambda sc: sc.activation(sb["row1"][:], pb[7][0:1, 0:SUB],
                                             AF.Ln, bias=sb["ceps6"][0:1, :], scale=1.0 / D),
                    reads_bank=7)
            self.snop()
            sc2 = self.so(lambda sc: sc.activation(sb["row2"][:], sb["row1"][:],
                                                   AF.Exp, scale=-0.5), inc=True)
            self.w("tensor", "sc", sc2)
            m2 = self.mm(6, pb[6][:, 0:SUB], sb["onesr"][:], sb["row2"][:], True, True)
            self.w("vector", "mm", m2)
            vvh = None
            wc = wcol if wcol is not None else 72
            for p in range(NPT):
                self.vo(lambda v, p=p: v.tensor_mul(
                    sb["zD"][:, p * SUB:(p + 1) * SUB],
                    sb["bigA"][:, p * SUB:(p + 1) * SUB], pb[6][:, 0:SUB]))
            for p in range(NPT):
                vvh = self.vo(lambda v, p=p, wc=wc: v.tensor_scalar_mul(
                    sb["hatR"][:, p * SUB:(p + 1) * SUB],
                    sb["zD"][:, p * SUB:(p + 1) * SUB],
                    sb["vsm"][:, wc + p:wc + p + 1]), inc=(p == NPT - 1))
            self.vo(lambda v: v.engine_nop(), reads_bank=6)
            self.store(self.dsub(out_d, s), sb["hatR"][:], after_vv=vvh)

    def st_lnrms(self, src_d, xl_d, h0_d):
        sb, dr, pb = self.sb, self.dr, self.pb
        self.stage_barrier()
        for s in range(NSUB):
            self.stage_barrier()
            lds = self.load(sb["bigA"][:], self.dsub(src_d, s))
            self.w("scalar", "ld", lds)
            self.w("tensor", "ld", lds)
            for p in range(NPT):
                self.mm(6 if p == 0 else None, pb[6][0:1, 0:SUB], sb["onesc"][:],
                        sb["bigA"][:, p * SUB:(p + 1) * SUB],
                        start=(p == 0), stop=(p == NPT - 1))
            mqs = None
            mm_hist = []
            for p in range(NPT):
                sl = "s1" if p % 2 == 0 else "s2"
                if p >= 2:
                    self.w("scalar", "mm", mm_hist[p - 2])
                scs = self.so(lambda sc, p=p, sl=sl: sc.activation(
                    sb[sl][:], sb["bigA"][:, p * SUB:(p + 1) * SUB], AF.Square),
                    inc=True)
                self.w("tensor", "sc", scs)
                mqs = self.mm(7 if p == 0 else None, pb[7][0:1, 0:SUB], sb["onesc"][:],
                              sb[sl][:], start=(p == 0), stop=(p == NPT - 1), inc=True)
                mm_hist.append(mqs)
            self.w("vector", "mm", mqs)
            self.vo(lambda v: v.tensor_scalar_mul(sb["row1"][:], pb[6][0:1, 0:SUB], 1.0 / D))
            self.vo(lambda v: v.tensor_scalar_mul(sb["row3"][:], pb[7][0:1, 0:SUB], 1.0 / D),
                    reads_bank=[6, 7])
            self.vo(lambda v: v.tensor_mul(sb["row2"][:], sb["row1"][:], sb["row1"][:]))
            self.vnop()
            vv1 = self.vo(lambda v: v.tensor_sub(sb["row2"][:], sb["row3"][:], sb["row2"][:]),
                          inc=True)
            self.w("scalar", "vv", vv1)
            self.so(lambda sc: sc.activation(sb["row3"][:], sb["row2"][:], AF.Ln,
                                             bias=sb["ceps5"][0:1, :]))
            self.snop()
            self.so(lambda sc: sc.activation(sb["s3"][0:1, 0:SUB], sb["row3"][:],
                                             AF.Exp, scale=-0.5))
            self.so(lambda sc: sc.activation(sb["row3"][:], sb["row2"][:], AF.Ln,
                                             bias=sb["ceps11"][0:1, :], scale=1.0 + 1e-6))
            self.snop()
            sc2 = self.so(lambda sc: sc.activation(sb["row2"][:], sb["row3"][:],
                                                   AF.Exp, scale=-0.5), inc=True)
            self.w("tensor", "sc", sc2)
            m1 = self.mm(6, pb[6][:, 0:SUB], sb["onesr"][:], sb["row1"][:], True, True)
            m2 = self.mm(7, pb[7][:, 0:SUB], sb["onesr"][:], sb["s3"][0:1, 0:SUB], True, True)
            m3 = self.mm(5, pb[5][:, 0:SUB], sb["onesr"][:], sb["row2"][:], True, True)
            self.w("vector", "mm", m3)
            for p in range(NPT):
                self.vo(lambda v, p=p: v.tensor_sub(
                    sb["zD"][:, p * SUB:(p + 1) * SUB],
                    sb["bigA"][:, p * SUB:(p + 1) * SUB], pb[6][:, 0:SUB]))
            vvx = None
            for p in range(NPT):
                vvx = self.vo(lambda v, p=p: v.tensor_mul(
                    sb["hatR"][:, p * SUB:(p + 1) * SUB],
                    sb["zD"][:, p * SUB:(p + 1) * SUB], pb[7][:, 0:SUB]),
                    inc=(p == NPT - 1))
            vvh = None
            for p in range(NPT):
                vvh = self.vo(lambda v, p=p: v.scalar_tensor_tensor(
                    sb["hidR"][:, p * SUB:(p + 1) * SUB],
                    sb["zD"][:, p * SUB:(p + 1) * SUB],
                    sb["vsm"][:, p:p + 1], pb[5][:, 0:SUB], OP.mult, OP.mult),
                    inc=(p == NPT - 1))
            self.vo(lambda v: v.engine_nop(), reads_bank=[5, 6, 7])
            self.store(self.dsub(xl_d, s), sb["hatR"][:], after_vv=vvx)
            self.store(self.dsub(h0_d, s).bitcast(F32), sb["hidR"][:].bitcast(F32),
                       after_vv=vvh)

    def st_projA(self):
        sb, dr, pb = self.sb, self.dr, self.pb
        self.stage_barrier()
        ldw = 0
        for kc in range(8):
            ldw = self.load(sb["wbufA"][:, kc * 512:(kc + 1) * 512],
                            dr["wk"][kc * 128:(kc + 1) * 128, :])
        for kc in range(8):
            ldw = self.load(sb["wbufA"][:, 4096 + kc * 512:4096 + (kc + 1) * 512],
                            dr["wq"][kc * 128:(kc + 1) * 128, :])
        for kc in range(8):
            ldw = self.load(sb["wbufA"][:, 8192 + kc * 512:8192 + (kc + 1) * 512],
                            dr["w_spike"][kc * 128:(kc + 1) * 128, :])
        for kc in range(8):
            ldw = self.load(sb["wbufA"][:, 12288 + kc * 64:12288 + (kc + 1) * 64],
                            dr["w_au"][kc * 128:(kc + 1) * 128, :])
        ldw = self.load(sb["wbufA"][0:64, 12800:13312], dr["w_ad"][:])
        for kc in range(8):
            ldw = self.load(sb["wbufA"][:, 13312 + kc * 8:13312 + (kc + 1) * 8].bitcast(F32),
                            dr["w_beta"][kc * 128:(kc + 1) * 128, :])
        self.w("tensor", "ld", ldw)
        for s in range(NSUB):
            self.stage_barrier()
            lds = self.load(sb["hatR"][:], self.dsub(dr["d_xha"], s))
            self.w("vector", "ld", lds)
            vc = None
            for p in range(NPT):
                xs = lambda lo, hi: sb["hatR"][:, p * SUB + lo: p * SUB + hi].bitcast(F32)
                self.vo(lambda v, p=p, xs=xs: v.tensor_scalar_mul(
                    sb["cacc"][:, p * SUB:(p + 1) * SUB], xs(0, SUB),
                    sb["vsm"][:, 19 + p * 4:20 + p * 4]))
                for j in (0, 1, 2):
                    off = 3 - j
                    self.vo(lambda v, p=p, j=j, off=off, xs=xs: v.scalar_tensor_tensor(
                        sb["cacc"][:, p * SUB + off:(p + 1) * SUB],
                        xs(0, SUB - off),
                        sb["vsm"][:, 16 + p * 4 + j:17 + p * 4 + j],
                        sb["cacc"][:, p * SUB + off:(p + 1) * SUB], OP.mult, OP.add))
                    vc = self.vo(lambda v, p=p, j=j, off=off: v.scalar_tensor_tensor(
                        sb["cacc"][:, p * SUB:p * SUB + off],
                        sb["halo"][:, p * 3 + j:p * 3 + j + off],
                        sb["vsm"][:, 16 + p * 4 + j:17 + p * 4 + j],
                        sb["cacc"][:, p * SUB:p * SUB + off], OP.mult, OP.add),
                        inc=(p == NPT - 1 and j == 2))
            vhal = None
            for p in range(NPT):
                vhal = self.vo(lambda v, p=p: v.tensor_copy(
                    sb["halo"][:, p * 3:p * 3 + 3],
                    sb["hatR"][:, (p + 1) * SUB - 3:(p + 1) * SUB].bitcast(F32)),
                    inc=(p == NPT - 1))
            self.w("scalar", "vv", vc)
            sch = None
            for p in range(NPT):
                sch = self.so(lambda sc, p=p: sc.activation(
                    sb["hidR"][:, p * SUB:(p + 1) * SUB],
                    sb["cacc"][:, p * SUB:(p + 1) * SUB], AF.Silu,
                    bias=sb["vsm"][:, 8 + p:9 + p]), inc=(p == NPT - 1))
            self.store(self.dsub(dr["d_h"], s), sb["hidR"][:], after_vv=vhal,
                       after_sc=sch)
            self.w("tensor", "sc", sch)
            hap = lambda kc: sb["hidR"][:, kc * SUB:(kc + 1) * SUB]

            def proj4(dst, wofs, spill_d):
                vvl = None
                for oc in range(4):
                    bk = oc % 4
                    mo = None
                    for kc in range(8):
                        mo = self.mm(bk if kc == 0 else None, pb[bk][:, 0:SUB],
                                     sb["wbufA"][:, wofs + kc * 512 + oc * 128:
                                                 wofs + kc * 512 + (oc + 1) * 128],
                                     hap(kc), start=(kc == 0), stop=(kc == 7))
                    self.w("vector", "mm", mo)
                    vvl = self.vo(lambda v, oc=oc, bk=bk, dst=dst: v.tensor_copy(
                        dst[:, oc * SUB:(oc + 1) * SUB], pb[bk][:, 0:SUB]),
                        inc=True, reads_bank=bk)
                self.store(self.dsub(spill_d, s, npt=4), dst[:, 0:4 * SUB],
                           after_vv=vvl)
                return vvl

            proj4(sb["kprj"], 0, dr["d_k"])
            proj4(sb["qprj"], 4096, dr["d_q"])
            self.w("scalar", "vv", self.cnt["vv"])
            mh = []
            for (src, dsum) in ((sb["kprj"], dr["d_kss"]), (sb["qprj"], dr["d_qss"])):
                dstrows = sb["kssA"] if dsum is dr["d_kss"] else sb["qssA"]
                vvk = None
                for p in range(4):
                    sl = "s1" if p % 2 == 0 else "s2"
                    if len(mh) >= 2:
                        self.w("scalar", "mm", mh[-2])
                    scv = self.so(lambda sc, p=p, sl=sl, src=src: sc.activation(
                        sb[sl][:], src[:, p * SUB:(p + 1) * SUB].bitcast(F32),
                        AF.Square), inc=True)
                    self.w("tensor", "sc", scv)
                    mo = self.mm(4 + p, pb[4 + p][0:2, 0:SUB],
                                 sb["blk2T"][:, 2 * p:2 * p + 2],
                                 sb[sl][:], start=True, stop=True)
                    mh.append(mo)
                    self.w("vector", "mm", mo)
                    vvk = self.vo(lambda v, p=p, dstrows=dstrows: v.tensor_copy(
                        dstrows[:, p * SUB:(p + 1) * SUB], pb[4 + p][0:2, 0:SUB]),
                        inc=True, reads_bank=4 + p)
                self.store(dsum[:, s * 4 * SUB:(s + 1) * 4 * SUB], dstrows[:],
                           after_vv=vvk)
            vvl = None
            for oc in range(4):
                bk = oc % 4
                mo = None
                for kc in range(8):
                    mo = self.mm(bk if kc == 0 else None, pb[bk][:, 0:SUB],
                                 sb["wbufA"][:, 8192 + kc * 512 + oc * 128:
                                             8192 + kc * 512 + (oc + 1) * 128],
                                 hap(kc), start=(kc == 0), stop=(kc == 7))
                self.w("vector", "mm", mo)
                vvl = self.vo(lambda v, oc=oc, bk=bk: v.tensor_copy(
                    sb["sdP"][:, oc * SUB:(oc + 1) * SUB],
                    pb[bk][:, 0:SUB]), inc=True, reads_bank=bk)
            self.store(self.dsub(dr["d_sd"], s, npt=4), sb["sdP"][:], after_vv=vvl)
            mo = None
            for kc in range(8):
                mo = self.mm(5 if kc == 0 else None, pb[5][0:RL, 0:SUB],
                             sb["wbufA"][:, 12288 + kc * 64:12288 + (kc + 1) * 64],
                             hap(kc), start=(kc == 0), stop=(kc == 7))
            self.w("scalar", "mm", mo)
            scau = self.so(lambda sc: sc.activation(
                sb["s3"][0:RL, 0:SUB], pb[5][0:RL, 0:SUB], AF.Silu,
                bias=sb["bau"][:]), inc=True, reads_bank=5)
            self.w("tensor", "sc", scau)
            vvl = None
            for oc in range(4):
                bk = oc % 4
                mo = self.mm(bk, pb[bk][:, 0:SUB],
                             sb["wbufA"][0:64, 12800 + oc * 128:12800 + (oc + 1) * 128].bitcast(F32),
                             sb["s3"][0:RL, 0:SUB], True, True)
                self.w("vector", "mm", mo)
                vvl = self.vo(lambda v, oc=oc, bk=bk: v.tensor_scalar_add(
                    sb["abP"][:, oc * SUB:(oc + 1) * SUB], pb[bk][:, 0:SUB],
                    sb["vsm"][:, 48 + oc:49 + oc]), inc=True, reads_bank=bk)
            self.store(self.dsub(dr["d_ab"], s, npt=4), sb["abP"][:], after_vv=vvl)
            mo = None
            for kc in range(8):
                mo = self.mm(5 if kc == 0 else None, pb[5][0:H, 0:SUB],
                             sb["wbufA"][:, 13312 + kc * 8:13312 + (kc + 1) * 8].bitcast(F32),
                             hap(kc).bitcast(F32), start=(kc == 0), stop=(kc == 7))
            self.w("vector", "mm", mo)
            vvl = self.vo(lambda v: v.tensor_scalar_add(
                sb["r8a"][:], pb[5][0:H, 0:SUB], sb["bbeta"][:]),
                inc=True, reads_bank=5)
            self.store(dr["d_bb"][:, s * SUB:(s + 1) * SUB], sb["r8a"][:], after_vv=vvl)

    def st_projB(self):
        sb, dr, pb = self.sb, self.dr, self.pb
        self.stage_barrier()
        ldw = 0
        for kc in range(8):
            ldw = self.load(sb["wbufA"][:, kc * 1024:(kc + 1) * 1024],
                            dr["wv"][kc * 128:(kc + 1) * 128, :])
        self.w("tensor", "ld", ldw)
        for s in range(NSUB):
            self.stage_barrier()
            lds = self.load(sb["hidR"][:], self.dsub(dr["d_h"], s))
            self.w("tensor", "ld", lds)
            for hh in range(8):
                bk = 4 + hh % 4
                mo = None
                for kc in range(8):
                    mo = self.mm(bk if kc == 0 else None, pb[bk][:, 0:SUB],
                                 sb["wbufA"][:, kc * 1024 + hh * 128: kc * 1024 + (hh + 1) * 128],
                                 sb["hidR"][:, kc * SUB:(kc + 1) * SUB],
                                 start=(kc == 0), stop=(kc == 7))
                self.w("vector", "mm", mo)
                dstv = sb["vsub"].rearrange("r (c h u) -> r c h u", c=NCH, h=H)[:, :, hh, :]
                srcv = pb[bk][:, 0:SUB].rearrange("r (c u) -> r c u", c=NCH)
                self.vo(lambda v, dstv=dstv, srcv=srcv: v.tensor_copy(dstv, srcv),
                        inc=True, reads_bank=bk)
            self.store(dr["d_v"][:, s * H * SUB:(s + 1) * H * SUB], sb["vsub"][:],
                       after_vv=self.cnt["vv"])

    def st_scan(self):
        sb, dr, pb = self.sb, self.dr, self.pb
        self.stage_barrier()
        for s in range(NSUB):
            self.stage_barrier()
            lds = self.load(sb["sdS"][:], self.dsub(dr["d_sd"], s, npt=4))
            self.w("vector", "ld", lds)
            self.vo(lambda v: v.memset(sb["s1"][:], DEC))
            self.vnop()
            for p in range(4):
                self.vo(lambda v, p=p: v.tensor_tensor_scan(
                    sb["psc"][:, p * SUB:(p + 1) * SUB], sb["s1"][:],
                    sb["sdS"][:, p * SUB:(p + 1) * SUB],
                    sb["pcar"][:, p:p + 1], OP.mult, OP.add))
            for p in range(4):
                self.vo(lambda v, p=p: v.tensor_copy(
                    sb["pcar"][:, p:p + 1], sb["psc"][:, (p + 1) * SUB - 1:(p + 1) * SUB]))
            for p in range(4):
                self.vo(lambda v, p=p: v.tensor_scalar(
                    sb["ctil"][:, p * SUB:(p + 1) * SUB],
                    sb["psc"][:, p * SUB:(p + 1) * SUB], 2.0, -1.0, OP.mult, OP.add))
            for p in range(4):
                self.vo(lambda v, p=p: v.tensor_mul(
                    sb["ctil"][:, p * SUB:(p + 1) * SUB],
                    sb["ctil"][:, p * SUB:(p + 1) * SUB], sb["powt"][:]))
            for t in range(SUB):
                tau = t % CH
                gam = float(DEC ** (-tau))
                pa, qa = (0, 4) if t % 2 == 0 else (4, 0)
                for pair in ((0, 2), (1, 3)):
                    for blk in pair:
                        self.vo(lambda v, blk=blk, pa=pa, t=t: v.tensor_tensor(
                            sb["spk"][:, blk * SUB + t:blk * SUB + t + 1],
                            sb["Gst"][:, pa + blk:pa + blk + 1],
                            sb["ctil"][:, blk * SUB + t:blk * SUB + t + 1], OP.is_lt))
                    for blk in pair:
                        self.vo(lambda v, blk=blk, pa=pa, qa=qa, t=t, gam=gam:
                                v.scalar_tensor_tensor(
                                    sb["Gst"][:, qa + blk:qa + blk + 1],
                                    sb["spk"][:, blk * SUB + t:blk * SUB + t + 1], gam,
                                    sb["Gst"][:, pa + blk:pa + blk + 1], OP.mult, OP.add))
                if tau == CH - 1:
                    rs = float(DEC ** CH)
                    self.vo(lambda v, qa=qa, rs=rs: v.tensor_scalar_mul(
                        sb["Gst"][:, qa:qa + 2], sb["Gst"][:, qa:qa + 2], rs))
                    self.vo(lambda v, qa=qa, rs=rs: v.tensor_scalar_mul(
                        sb["Gst"][:, qa + 2:qa + 4], sb["Gst"][:, qa + 2:qa + 4], rs))
            vvd = self.vo(lambda v: v.engine_nop(), inc=True)
            self.store(self.dsub(dr["d_spk"], s, npt=4), sb["spk"][:], after_vv=vvd)

    def st_delta(self):
        sb, dr, pb = self.sb, self.dr, self.pb
        self.stage_barrier()
        self.vo(lambda v: v.memset(sb["bd_ap"][:].bitcast(F32), 0.0))
        self.vo(lambda v: v.memset(sb["bd_apn"][:].bitcast(F32), 0.0))
        self.vo(lambda v: v.memset(sb["bd_b"][:].bitcast(F32), 0.0), inc=True)
        for s in range(NSUB):
            self.stage_barrier()
            lds = self.load(sb["ksub"][:].bitcast(F32), self.dsub(dr["d_k"], s, npt=4).bitcast(F32))
            lds = self.load(sb["qsub"][:].bitcast(F32), self.dsub(dr["d_q"], s, npt=4).bitcast(F32))
            lds = self.load(sb["vsub"][:].bitcast(F32), dr["d_v"][:, s * H * SUB:(s + 1) * H * SUB].bitcast(F32))
            lds = self.load(sb["absub"][:], self.dsub(dr["d_ab"], s, npt=4))
            lds = self.load(sb["spk"][:], self.dsub(dr["d_spk"], s, npt=4))
            lds = self.load(sb["kssE"][:], dr["d_kss"][0:1, s * 4 * SUB:s * 4 * SUB + 2048])
            lds = self.load(sb["kssO"][:], dr["d_kss"][1:2, s * 4 * SUB:s * 4 * SUB + 2048])
            lds = self.load(sb["qssE"][:], dr["d_qss"][0:1, s * 4 * SUB:s * 4 * SUB + 2048])
            lds = self.load(sb["qssO"][:], dr["d_qss"][1:2, s * 4 * SUB:s * 4 * SUB + 2048])
            self.w("tensor", "ld", lds)
            self.w("vector", "ld", lds)
            self.w("scalar", "ld", lds)
            scn = None
            for nm in ("kssEO", "qssEO"):
                self.so(lambda sc, nm=nm: sc.activation(sb[nm][:], sb[nm][:], AF.Ln))
                self.snop()
                scn = self.so(lambda sc, nm=nm: sc.activation(
                    sb[nm][:], sb[nm][:], AF.Exp, scale=0.5), inc=True)
            self.w("vector", "sc", scn)
            for nm in ("kssEO", "qssEO"):
                self.vo(lambda v, nm=nm: v.tensor_scalar_add(sb[nm][:], sb[nm][:], 1e-6))
                self.vnop()
                self.vo(lambda v, nm=nm: v.reciprocal(sb[nm][:], sb[nm][:]), inc=True)
            self.w("tensor", "vv", self.cnt["vv"])
            vvn = None
            for p in range(4):
                m6 = self.mm(6, pb[6][:, 0:SUB], sb["bexpA"][:],
                             sb["kssE"][:, p * SUB:(p + 1) * SUB], True, False)
                m6 = self.mm(None, pb[6][:, 0:SUB], sb["bexpB"][:],
                             sb["kssO"][:, p * SUB:(p + 1) * SUB], False, True)
                m5 = self.mm(5, pb[5][:, 0:SUB], sb["bexpA"][:],
                             sb["qssE"][:, p * SUB:(p + 1) * SUB], True, False)
                m5 = self.mm(None, pb[5][:, 0:SUB], sb["bexpB"][:],
                             sb["qssO"][:, p * SUB:(p + 1) * SUB], False, True)
                self.w("vector", "mm", m5)
                self.vo(lambda v, p=p: v.tensor_tensor(
                    sb["ksub"][:, p * SUB:(p + 1) * SUB],
                    sb["ksub"][:, p * SUB:(p + 1) * SUB].bitcast(F32),
                    pb[6][:, 0:SUB], OP.mult), reads_bank=6)
                vvn = self.vo(lambda v, p=p: v.tensor_tensor(
                    sb["qsub"][:, p * SUB:(p + 1) * SUB],
                    sb["qsub"][:, p * SUB:(p + 1) * SUB].bitcast(F32),
                    pb[5][:, 0:SUB], OP.mult), inc=True, reads_bank=5)
            if DELTA_DBG and s == 0:
                cv = self.cnt["vv"]
                self.store(dr["d_dbg_ks"][:], sb["kssEO"][:], after_vv=cv)
                self.store(dr["d_dbg_qs"][:], sb["qssEO"][:], after_vv=cv)
                self.w("vector", "sp", self.cnt["sp"])
            mo = None
            for hh in range(H):
                p = hh // 2
                colA = p * 8 + hh
                colB = 32 + p * 8 + hh
                bkA = 5 if hh % 2 == 0 else 6
                moA = self.mm(bkA, pb[bkA][0:1, 0:SUB],
                              sb["slhs"][:, colA:colA + 1],
                              sb["spk"][:, p * SUB:(p + 1) * SUB], True, True)
                mo = self.mm(7 if hh % 2 == 0 else 3, pb[7 if hh % 2 == 0 else 3][0:1, 0:SUB],
                             sb["slhs"][:, colB:colB + 1],
                             sb["spk"][:, p * SUB:(p + 1) * SUB], True, True)
                self.w("vector", "mm", mo)
                self.vo(lambda v, hh=hh, bkA=bkA: v.tensor_copy(
                    sb["arow"][:, hh * SUB:(hh + 1) * SUB], pb[bkA][0:1, 0:SUB]),
                    reads_bank=bkA)
                self.vo(lambda v, hh=hh, bkB=(7 if hh % 2 == 0 else 3): v.tensor_copy(
                    sb["brow"][:, hh * SUB:(hh + 1) * SUB], pb[bkB][0:1, 0:SUB]),
                    inc=True, reads_bank=(7 if hh % 2 == 0 else 3))
            vsum = self.cnt["vv"]
            self.w("gpsimd", "vv", vsum)
            nbb = self.inc("sp", 16)
            if self.on("gpsimd"):
                self.eng.dma_start(
                    out=sb["brow"][:].rearrange("r (h t) -> r h t", h=H),
                    in_=dr["d_bb"][:, s * SUB:(s + 1) * SUB],
                    accum_op=OP.add).then_inc(self.sems["sp"], 16)
            self.w("vector", "sp", nbb)
            self.w("scalar", "sp", nbb)
            self.vo(lambda v: v.tensor_single_scalar(sb["arow"][:], sb["arow"][:],
                                                     0.0, OP.is_gt), inc=True)
            vvz = self.cnt["vv"]
            self.w("scalar", "vv", vvz)
            scb = self.so(lambda sc: sc.activation(sb["brow"][:], sb["brow"][:],
                                                   AF.Exp, scale=-1.0), inc=True)
            self.w("vector", "sc", scb)
            self.vo(lambda v: v.tensor_scalar_add(sb["brow"][:], sb["brow"][:], 1.0))
            self.vnop()
            self.vo(lambda v: v.reciprocal(sb["brow"][:], sb["brow"][:]))
            self.vnop()
            self.vo(lambda v: v.tensor_mul(sb["brow"][:], sb["brow"][:], sb["arow"][:]),
                    inc=True)
            vva = None
            for p in range(4):
                vva = self.vo(lambda v, p=p: v.scalar_tensor_tensor(
                    sb["absub"][:, p * SUB:(p + 1) * SUB],
                    sb["spk"][:, p * SUB:(p + 1) * SUB],
                    sb["vsm"][:, 52 + p:53 + p],
                    sb["absub"][:, p * SUB:(p + 1) * SUB], OP.mult, OP.add),
                    inc=(p == 3))
            self.w("scalar", "vv", vva)
            sca = None
            for p in range(4):
                sca = self.so(lambda sc, p=p: sc.activation(
                    sb["gt"][:, p * SUB:(p + 1) * SUB],
                    sb["absub"][:, p * SUB:(p + 1) * SUB], AF.Exp, scale=-1.0),
                    inc=(p == 3))
            self.w("vector", "sc", sca)
            for p in range(4):
                self.vo(lambda v, p=p: v.tensor_scalar_add(
                    sb["gt"][:, p * SUB:(p + 1) * SUB],
                    sb["gt"][:, p * SUB:(p + 1) * SUB], 1.0))
            self.vnop()
            for p in range(4):
                self.vo(lambda v, p=p: v.reciprocal(
                    sb["absub"][:, p * SUB:(p + 1) * SUB],
                    sb["gt"][:, p * SUB:(p + 1) * SUB]))
            self.vnop()
            self.w("tensor", "vv", vvz)
            for p in range(4):
                mo = self.mm(6, pb[6][:, 0:SUB], sb["bexpA"][:],
                             sb["arow"][:, (2 * p) * SUB:(2 * p + 1) * SUB], True, False)
                mo = self.mm(None, pb[6][:, 0:SUB], sb["bexpB"][:],
                             sb["arow"][:, (2 * p + 1) * SUB:(2 * p + 2) * SUB], False, True)
                self.w("vector", "mm", mo)
                sl = "s1" if p % 2 == 0 else "s2"
                self.vo(lambda v, sl=sl: v.tensor_scalar(
                    sb[sl][:], pb[6][:, 0:SUB], -1.0, 1.0, OP.mult, OP.add))
                self.vo(lambda v, p=p: v.tensor_mul(
                    sb["absub"][:, p * SUB:(p + 1) * SUB],
                    sb["absub"][:, p * SUB:(p + 1) * SUB], pb[6][:, 0:SUB]),
                    inc=True, reads_bank=6)
                if p > 0:
                    slp = "s1" if (p - 1) % 2 == 0 else "s2"
                    self.vo(lambda v, p=p, slp=slp: v.tensor_add(
                        sb["absub"][:, (p - 1) * SUB:p * SUB],
                        sb["absub"][:, (p - 1) * SUB:p * SUB], sb[slp][:]))
            self.vnop()
            self.vo(lambda v: v.tensor_add(
                sb["absub"][:, 3 * SUB:4 * SUB],
                sb["absub"][:, 3 * SUB:4 * SUB], sb["s2"][:]), inc=True)
            self.w("tensor", "vv", self.cnt["vv"])
            for hh in range(H):
                mo = self.mm(6, pb[6][:, 0:SUB], sb["onesr"][:],
                             sb["brow"][:, hh * SUB:(hh + 1) * SUB], True, True)
                self.w("vector", "mm", mo)
                dstb = sb["vsub"].rearrange("r (c h u) -> r c h u", c=NCH, h=H)[:, :, hh, :]
                srcb = pb[6][:, 0:SUB].rearrange("r (c u) -> r c u", c=NCH)
                self.vo(lambda v, dstb=dstb, srcb=srcb: v.tensor_tensor(
                    dstb, dstb.bitcast(F32), srcb, OP.mult),
                    inc=True, reads_bank=6)
            self.w("scalar", "vv", self.cnt["vv"])
            scl = None
            for p in range(4):
                scl = self.so(lambda sc, p=p: sc.activation(
                    sb["Lt"][:, p * SUB:(p + 1) * SUB],
                    sb["absub"][:, p * SUB:(p + 1) * SUB], AF.Ln), inc=(p == 3))
            self.vo(lambda v: v.memset(sb["s1"][:], 1.0))
            self.vnop()
            self.w("vector", "sc", scl)
            for p in range(4):
                for c in range(NCH):
                    self.vo(lambda v, p=p, c=c: v.tensor_tensor_scan(
                        sb["absub"][:, p * SUB + c * CH:p * SUB + (c + 1) * CH],
                        sb["s1"][:, 0:CH],
                        sb["Lt"][:, p * SUB + c * CH:p * SUB + (c + 1) * CH],
                        sb["zcol"][:], OP.mult, OP.add))
            vvL = self.vo(lambda v: v.engine_nop(), inc=True)
            self.w("scalar", "vv", vvL)
            scg = None
            for p in range(4):
                self.so(lambda sc, p=p: sc.activation(
                    sb["gt"][:, p * SUB:(p + 1) * SUB],
                    sb["absub"][:, p * SUB:(p + 1) * SUB], AF.Exp))
                scg = self.so(lambda sc, p=p: sc.activation(
                    sb["gti"][:, p * SUB:(p + 1) * SUB],
                    sb["absub"][:, p * SUB:(p + 1) * SUB], AF.Exp, scale=-1.0),
                    inc=(p == 3))
            self.w("vector", "sc", scg)
            vvb = None
            for p in range(4):
                m7 = self.mm(7, pb[7][:, 0:SUB], sb["bexpA"][:],
                             sb["brow"][:, (2 * p) * SUB:(2 * p + 1) * SUB], True, False)
                m7 = self.mm(None, pb[7][:, 0:SUB], sb["bexpB"][:],
                             sb["brow"][:, (2 * p + 1) * SUB:(2 * p + 2) * SUB], False, True)
                self.w("vector", "mm", m7)
                self.vo(lambda v, p=p: v.tensor_mul(
                    sb["s1"][:], sb["ksub"][:, p * SUB:(p + 1) * SUB].bitcast(F32),
                    sb["gt"][:, p * SUB:(p + 1) * SUB]))
                self.vo(lambda v, p=p: v.tensor_mul(
                    sb["bbv"][:, p * SUB:(p + 1) * SUB],
                    sb["ksub"][:, p * SUB:(p + 1) * SUB].bitcast(F32),
                    sb["gti"][:, p * SUB:(p + 1) * SUB]))
                self.vo(lambda v, p=p: v.tensor_mul(
                    sb["aap"][:, p * SUB:(p + 1) * SUB], sb["s1"][:],
                    pb[7][:, 0:SUB]), inc=True, reads_bank=7)
                self.vo(lambda v, p=p: v.tensor_mul(
                    sb["qtt"][:, p * SUB:(p + 1) * SUB],
                    sb["qsub"][:, p * SUB:(p + 1) * SUB].bitcast(F32),
                    sb["gt"][:, p * SUB:(p + 1) * SUB]))
                vvb = self.vo(lambda v, p=p: v.tensor_scalar_mul(
                    sb["aapn"][:, p * SUB:(p + 1) * SUB],
                    sb["aap"][:, p * SUB:(p + 1) * SUB].bitcast(F32), -1.0),
                    inc=(p == 3))
            self.w("tensor", "vv", vvb)
            if DELTA_DBG and s == 0:
                cv = self.cnt["vv"]
                self.store(self.dsub(dr["d_dbg_kn"], 0, npt=4),
                           sb["ksub"][:].bitcast(F32), after_vv=cv)
                self.store(self.dsub(dr["d_dbg_al"], 0, npt=4), sb["absub"][:],
                           after_vv=cv)
                self.store(self.dsub(dr["d_dbg_gt"], 0, npt=4), sb["gt"][:],
                           after_vv=cv)
                self.store(self.dsub(dr["d_dbg_gti"], 0, npt=4), sb["gti"][:],
                           after_vv=cv)
                self.store(self.dsub(dr["d_dbg_aap"], 0, npt=4),
                           sb["aap"][:].bitcast(F32), after_vv=cv)
                self.store(self.dsub(dr["d_dbg_bbv"], 0, npt=4),
                           sb["bbv"][:].bitcast(F32), after_vv=cv)
                self.store(self.dsub(dr["d_dbg_qtt"], 0, npt=4),
                           sb["qtt"][:].bitcast(F32), after_vv=cv)
                self.store(dr["d_dbg_vt"][:], sb["vsub"][:].bitcast(F32),
                           after_vv=cv)
                self.store(dr["d_dbg_ar"][:], sb["arow"][:], after_vv=cv)
                self.store(dr["d_dbg_br"][:], sb["brow"][:], after_vv=cv)
            for c in range(NCH):
                self._delta_chunk(s, c)
            self.store(dr["d_y"][:, s * H * SUB:(s + 1) * H * SUB],
                       sb["ysub"][:], after_vv=self.cnt["vv"])

    def _sumlhs(self, p):
        return self.sb["slhs"][:, p * 8:p * 8 + 8]

    def _bslhs(self, p):
        return self.sb["slhs"][:, 32 + p * 8:32 + p * 8 + 8]

    def _build_sumlhs(self):
        sb = self.sb
        for p in range(4):
            b = p * 8
            self.vo(lambda v, b=b, p=p: v.tensor_copy(
                sb["slhs"][0:64, b + 2 * p:b + 2 * p + 1], sb["onesc"][0:64, :]))
            self.vo(lambda v, b=b, p=p: v.tensor_copy(
                sb["slhs"][64:128, b + 2 * p + 1:b + 2 * p + 2], sb["onesc"][64:128, :]))
            b2 = 32 + p * 8
            self.vo(lambda v, b2=b2, p=p: v.tensor_copy(
                sb["slhs"][0:64, b2 + 2 * p:b2 + 2 * p + 1],
                sb["vsm"][0:64, 56 + p:57 + p]))
            self.vo(lambda v, b2=b2, p=p: v.tensor_copy(
                sb["slhs"][64:128, b2 + 2 * p + 1:b2 + 2 * p + 2],
                sb["vsm"][64:128, 56 + p:57 + p]))

    def _delta_chunk(self, s, c):
        sb, pb = self.sb, self.pb
        for (srct, dstt) in (("aap", "bd_ap"), ("aapn", "bd_apn"), ("bbv", "bd_b")):
            src = sb[srct].rearrange("r (p x) -> r p x", p=4)[:, :, c * CH:(c + 1) * CH]
            dst = sb[dstt].rearrange("r (p x) -> r p x", p=4)
            self.vo(lambda v, src=src, dst=dst: v.tensor_copy(
                dst[0:64, :, 0:64], src[0:64, :, :]))
            self.vo(lambda v, src=src, dst=dst: v.tensor_copy(
                dst[64:128, :, 64:128], src[64:128, :, :]))
        vvbd = self.vo(lambda v: v.engine_nop(), inc=True)
        self.w("tensor", "vv", vvbd)
        for p in range(4):
            self._delta_pair(s, c, p)

    def _delta_pair(self, s, c, p):
        sb, pb = self.sb, self.pb
        he, ho = 2 * p, 2 * p + 1
        bd = lambda t: sb[t][:, p * 128:(p + 1) * 128]
        qsrc = sb["qtt"].rearrange("r (pp x) -> r pp x", pp=4)[:, p, c * CH:(c + 1) * CH]
        self.vo(lambda v, qsrc=qsrc: v.tensor_copy(sb["qp1"][0:64, :], qsrc[0:64, :]))
        vvq = self.vo(lambda v, qsrc=qsrc: v.tensor_copy(
            sb["qp2"][64:128, :], qsrc[64:128, :]), inc=True)
        bpair = sb["bbv"].rearrange("r (pp x) -> r pp x", pp=4)[:, p, c * CH:(c + 1) * CH]
        mW = self.mm(0, pb[0][:, 0:64], bd("bd_ap"), bpair, True, True)
        vpair = sb["vsub"][:, c * SUB + he * CH: c * SUB + he * CH + 128]
        self.mm(1, pb[1][:, 0:128], vpair.bitcast(F32), sb["ident"][:], True, False, tr=True)
        mR = self.mm(None, pb[1][:, 0:128], bd("bd_apn"),
                     sb["Spair"][:, p * 128:(p + 1) * 128], False, True)
        mM = self.mm(2, pb[2][:, 0:64], bd("bd_b"), qsrc, True, True)
        self.w("vector", "mm", mM)
        self.vo(lambda v: v.scalar_tensor_tensor(
            sb["V0"][0:64, 0:64], pb[0][0:64, 0:64], -1.0, sb["lt"][:], OP.mult, OP.mult))
        self.vo(lambda v: v.scalar_tensor_tensor(
            sb["V0"][64:128, 64:128], pb[0][64:128, 0:64], -1.0, sb["lt"][:], OP.mult, OP.mult),
            inc=True, reads_bank=0)
        self.vo(lambda v: v.tensor_copy(sb["Uk"][:], pb[1][:, 0:128]),
                inc=True, reads_bank=1)
        self.vo(lambda v: v.tensor_mul(sb["Mp1"][0:64, :], pb[2][0:64, 0:64], sb["ul"][:]))
        self.vo(lambda v: v.tensor_mul(sb["Mp2"][64:128, :], pb[2][64:128, 0:64], sb["ul"][:]),
                inc=True, reads_bank=2)
        self.w("tensor", "vv", self.cnt["vv"])
        self.vo(lambda v: v.memset(sb["V0"][0:64, 64:128], 0.0))
        self.vo(lambda v: v.memset(sb["V0"][64:128, 0:64], 0.0), inc=True)
        self.w("tensor", "vv", self.cnt["vv"])
        mT = self.mm(3, pb[3][:, 0:128], sb["V0"][:].bitcast(F32), sb["ident"][:],
                     True, True, tr=True)
        self.w("vector", "mm", mT)
        self.vo(lambda v: v.tensor_copy(sb["V0T"][:], pb[3][:, 0:128]),
                inc=True, reads_bank=3)
        self.w("tensor", "vv", self.cnt["vv"])
        Vc, VcT, Un = "V0", "V0T", "Uk"
        for k in range(6):
            mU = self.mm(4, pb[4][:, 0:128], sb[VcT][:], sb[Un][:], True, True)
            mlast = mU
            if k < 5:
                self.mm(5, pb[5][:, 0:128], sb[VcT][:], sb[Vc][:], True, True)
                mlast = self.mm(6, pb[6][:, 0:128], sb[Vc][:], sb[VcT][:], True, True)
            self.w("vector", "mm", mlast)
            Un2 = "Uk2" if Un == "Uk" else "Uk"
            self.vo(lambda v, Un=Un, Un2=Un2: v.tensor_add(
                sb[Un2][:], sb[Un][:].bitcast(F32), pb[4][:, 0:128]),
                inc=True, reads_bank=4)
            Un = Un2
            if k < 5:
                Vn = "Vk" if Vc != "Vk" else "V0"
                VnT = "VkT" if VcT != "VkT" else "V0T"
                self.vo(lambda v, Vn=Vn: v.tensor_copy(sb[Vn][:], pb[5][:, 0:128]),
                        inc=True, reads_bank=5)
                self.vo(lambda v, VnT=VnT: v.tensor_copy(sb[VnT][:], pb[6][:, 0:128]),
                        inc=True, reads_bank=6)
                Vc, VcT = Vn, VnT
            self.w("tensor", "vv", self.cnt["vv"])
        self.mm(7, pb[7][:, 0:64], sb["Spair"][:, p * 128:(p + 1) * 128],
                sb["qp1"][:], True, False)
        self.mm(None, pb[7][:, 0:64], sb[Un][:], sb["Mp1"][:], False, True)
        self.mm(None, pb[7][:, 64:128], sb["Spair"][:, p * 128:(p + 1) * 128],
                sb["qp2"][:], True, False)
        mO = self.mm(None, pb[7][:, 64:128], sb[Un][:], sb["Mp2"][:], False, True)
        self.w("vector", "mm", mO)
        self.vo(lambda v: v.tensor_copy(
            sb["ysub"][:, he * SUB + c * CH: he * SUB + (c + 1) * CH], pb[7][:, 0:64]))
        self.vo(lambda v: v.tensor_copy(
            sb["ysub"][:, ho * SUB + c * CH: ho * SUB + (c + 1) * CH], pb[7][:, 64:128]),
            inc=True, reads_bank=7)
        mBT = self.mm(0, pb[0][:, 0:128], bd("bd_b").bitcast(F32), sb["ident"][:],
                      True, True, tr=True)
        self.w("vector", "mm", mBT)
        self.vo(lambda v: v.tensor_copy(bd("bd_bt"), pb[0][:, 0:128]),
                inc=True, reads_bank=0)
        self.w("tensor", "vv", self.cnt["vv"])
        mDS = self.mm(1, pb[1][:, 0:128], bd("bd_bt"), sb[Un][:], True, True)
        self.w("vector", "mm", mDS)
        self.vo(lambda v: v.tensor_add(
            sb["Spair"][:, p * 128:(p + 1) * 128],
            sb["Spair"][:, p * 128:(p + 1) * 128].bitcast(F32), pb[1][:, 0:128]),
            reads_bank=1)
        self.vnop()
        gcol = sb["gt"][:, p * SUB + (c + 1) * CH - 1: p * SUB + (c + 1) * CH]
        self.vo(lambda v, gcol=gcol: v.tensor_scalar_mul(
            sb["Spair"][:, p * 128:(p + 1) * 128],
            sb["Spair"][:, p * 128:(p + 1) * 128].bitcast(F32), gcol), inc=True)
        self.w("tensor", "vv", self.cnt["vv"])

    def st_postattn(self):
        sb, dr, pb = self.sb, self.dr, self.pb
        self.stage_barrier()
        ldw = 0
        for kc in range(8):
            ldw = self.load(sb["wbufA"][:, kc * 1024:(kc + 1) * 1024],
                            dr["w_out"][kc * 128:(kc + 1) * 128, :])
        for kc in range(8):
            ldw = self.load(sb["wbufA"][:, 8192 + kc * 64:8192 + (kc + 1) * 64],
                            dr["w_u1"][kc * 128:(kc + 1) * 128, :])
        ldw = self.load(sb["wbufA"][0:RL, 8704:9728], dr["w_u2"][:])
        for s in range(NSUB):
            self.stage_barrier()
            lds = self.load(sb["ysub"][:], dr["d_y"][:, s * H * SUB:(s + 1) * H * SUB])
            lds = self.load(sb["bigA"][:], self.dsub(dr["d_xa"], s))
            lds = self.load(sb["xlsub"][:], self.dsub(dr["d_xl"], s).bitcast(F32))
            self.w("scalar", "ld", lds)
            self.w("tensor", "ld", max(lds, ldw))
            for hh in range(H):
                bk = 5 if hh % 2 == 0 else 6
                sl = "s1" if hh % 2 == 0 else "s2"
                scq = self.so(lambda sc, hh=hh, sl=sl: sc.activation(
                    sb[sl][:], sb["ysub"][:, hh * SUB:(hh + 1) * SUB], AF.Square),
                    inc=True)
                self.w("tensor", "sc", scq)
                mo = self.mm(bk, pb[bk][0:1, 0:SUB], sb["onesc"][:], sb[sl][:],
                             True, True)
                self.w("scalar", "mm", mo)
                self.so(lambda sc, bk=bk: sc.activation(
                    sb["row1"][:], pb[bk][0:1, 0:SUB], AF.Ln, bias=sb["ceps6"][0:1, :],
                    scale=1.0 / DV), reads_bank=bk)
                self.snop()
                scr = self.so(lambda sc: sc.activation(
                    sb["row2"][:], sb["row1"][:], AF.Exp, scale=-0.5), inc=True)
                self.w("tensor", "sc", scr)
                mb = self.mm(bk, pb[bk][:, 0:SUB], sb["onesr"][:], sb["row2"][:],
                             True, True)
                self.w("vector", "mm", mb)
                self.vo(lambda v, hh=hh, bk=bk: v.scalar_tensor_tensor(
                    sb["hidR"][:, hh * SUB:(hh + 1) * SUB],
                    sb["ysub"][:, hh * SUB:(hh + 1) * SUB],
                    sb["whn"][:, hh:hh + 1], pb[bk][:, 0:SUB], OP.mult, OP.mult),
                    inc=True, reads_bank=bk)
            vyn = self.cnt["vv"]
            self.w("tensor", "vv", vyn)
            mo = None
            for kc in range(8):
                mo = self.mm(5 if kc == 0 else None, pb[5][0:RL, 0:SUB],
                             sb["wbufA"][:, 8192 + kc * 64:8192 + (kc + 1) * 64].bitcast(F32),
                             sb["xlsub"][:, kc * SUB:(kc + 1) * SUB],
                             start=(kc == 0), stop=(kc == 7))
            self.w("scalar", "mm", mo)
            scz = self.so(lambda sc: sc.activation(
                sb["s3"][0:RL, :], pb[5][0:RL, 0:SUB], AF.Silu,
                bias=sb["bu1"][:]), inc=True, reads_bank=5)
            self.w("tensor", "sc", scz)
            for oc in range(8):
                bk = oc % 2 + 5
                mo = self.mm(bk, pb[bk][:, 0:SUB],
                             sb["wbufA"][0:RL, 8704 + oc * 128:8704 + (oc + 1) * 128].bitcast(F32),
                             sb["s3"][0:RL, :], True, True)
                self.w("scalar", "mm", mo)
                sl = "s1" if oc % 2 == 0 else "s2"
                sct = self.so(lambda sc, oc=oc, bk=bk, sl=sl: sc.activation(
                    sb[sl][:], pb[bk][0:128, 0:SUB],
                    AF.Tanh, bias=sb["vsm"][:, 64 + oc:65 + oc],
                    scale=0.5), inc=True, reads_bank=bk)
                self.w("vector", "sc", sct)
                self.vo(lambda v, oc=oc, sl=sl: v.tensor_scalar(
                    sb["gsub"][:, oc * SUB:(oc + 1) * SUB],
                    sb[sl][:], 0.5, 0.5, OP.mult, OP.add), inc=True)
            vg = self.cnt["vv"]
            self.w("tensor", "vv", vg)
            for oc in range(8):
                bk = oc % 4
                mo = None
                for kc in range(8):
                    mo = self.mm(bk if kc == 0 else None, pb[bk][:, 0:SUB],
                                 sb["wbufA"][:, kc * 1024 + oc * 128: kc * 1024 + (oc + 1) * 128],
                                 sb["hidR"][:, kc * SUB:(kc + 1) * SUB],
                                 start=(kc == 0), stop=(kc == 7))
                self.w("vector", "mm", mo)
                self.vo(lambda v, oc=oc, bk=bk: v.tensor_mul(
                    sb["hatR"][:, oc * SUB:(oc + 1) * SUB].bitcast(F32),
                    pb[bk][:, 0:SUB],
                    sb["gsub"][:, oc * SUB:(oc + 1) * SUB]), inc=True, reads_bank=bk)
            self.vnop()
            vyi = None
            for oc in range(8):
                vyi = self.vo(lambda v, oc=oc: v.tensor_add(
                    sb["ysub"][:, oc * SUB:(oc + 1) * SUB],
                    sb["hatR"][:, oc * SUB:(oc + 1) * SUB].bitcast(F32),
                    sb["xlsub"][:, oc * SUB:(oc + 1) * SUB]), inc=(oc == 7))
            self.store(self.dsub(dr["d_xha"], s).bitcast(F32), sb["ysub"][:],
                       after_vv=vyi)
            vx2 = None
            for oc in range(8):
                vx2 = self.vo(lambda v, oc=oc: v.tensor_add(
                    sb["hatR"][:, oc * SUB:(oc + 1) * SUB].bitcast(F32),
                    sb["ysub"][:, oc * SUB:(oc + 1) * SUB],
                    sb["bigA"][:, oc * SUB:(oc + 1) * SUB]), inc=(oc == 7))
            self.store(self.dsub(dr["d_x2"], s).bitcast(F32), sb["hatR"][:].bitcast(F32), after_vv=vx2)
            self.store(self.dsub(dr["d_x3"], s).bitcast(F32), sb["hatR"][:].bitcast(F32), after_vv=vx2)

    def st_ffnattn(self):
        sb, dr, pb = self.sb, self.dr, self.pb
        self.stage_barrier()
        NSLAB = 4
        SL = FF // NSLAB
        for sl_i in range(NSLAB):
            self.stage_barrier()
            ldw = 0
            for kc in range(8):
                ldw = self.load(sb["wbufA"][:, kc * 1024:(kc + 1) * 1024],
                                dr["w_ff1"][kc * 128:(kc + 1) * 128, sl_i * SL:(sl_i + 1) * SL])
            for kc in range(8):
                ldw = self.load(sb["wbufA"][:, 8192 + kc * 1024:8192 + (kc + 1) * 1024],
                                dr["w_ff3"][kc * 128:(kc + 1) * 128, sl_i * SL:(sl_i + 1) * SL])
            for kcb in range(8):
                ldw = self.load(sb["wbufB"][:, kcb * 1024:(kcb + 1) * 1024],
                                dr["w_ff2"][sl_i * SL + kcb * 128: sl_i * SL + (kcb + 1) * 128, :])
            for s in range(NSUB):
                self.stage_barrier()
                lds = self.load(sb["hatR"][:], self.dsub(dr["d_xh2"], s))
                self.w("tensor", "ld", max(lds, ldw))
                for j in range(8):
                    ba, bb_ = j % 4, 4 + j % 4
                    for kc in range(8):
                        self.mm(ba if kc == 0 else None, pb[ba][:, 0:SUB],
                                sb["wbufA"][:, kc * 1024 + j * 128: kc * 1024 + (j + 1) * 128],
                                sb["hatR"][:, kc * SUB:(kc + 1) * SUB],
                                start=(kc == 0), stop=(kc == 7))
                    mmb = None
                    for kc in range(8):
                        mmb = self.mm(bb_ if kc == 0 else None, pb[bb_][:, 0:SUB],
                                      sb["wbufA"][:, 8192 + kc * 1024 + j * 128: 8192 + kc * 1024 + (j + 1) * 128],
                                      sb["hatR"][:, kc * SUB:(kc + 1) * SUB],
                                      start=(kc == 0), stop=(kc == 7))
                    self.w("scalar", "mm", mmb)
                    sl = "s1" if j % 2 == 0 else "s2"
                    sc1 = self.so(lambda sc, sl=sl, ba=ba: sc.activation(
                        sb[sl][:], pb[ba][:, 0:SUB], AF.Silu), inc=True, reads_bank=ba)
                    self.w("vector", "sc", sc1)
                    self.vo(lambda v, j=j, sl=sl, bb_=bb_: v.tensor_mul(
                        sb["hidR"][:, j * SUB:(j + 1) * SUB],
                        sb[sl][:], pb[bb_][:, 0:SUB]), inc=True, reads_bank=bb_)
                self.w("tensor", "vv", self.cnt["vv"])
                for oc in range(8):
                    bk = oc % 4
                    mo = None
                    for kcb in range(8):
                        mo = self.mm(bk if kcb == 0 else None, pb[bk][:, 0:SUB],
                                     sb["wbufB"][:, kcb * 1024 + oc * 128: kcb * 1024 + (oc + 1) * 128],
                                     sb["hidR"][:, kcb * SUB:(kcb + 1) * SUB],
                                     start=(kcb == 0), stop=(kcb == 7))
                    self.w("vector", "mm", mo)
                    self.vo(lambda v, oc=oc, bk=bk: v.tensor_scalar_add(
                        sb["bigA"][:, oc * SUB:(oc + 1) * SUB], pb[bk][:, 0:SUB], 0.0),
                        inc=True, reads_bank=bk)
                self.store(self.dsub(dr["d_x3"], s), sb["bigA"][:],
                           after_vv=self.cnt["vv"], accum=True)
